# version 03 (final)
# baseline (speedup 1.0000x reference)
import numpy as np
import ml_dtypes
import concourse.bass as bass
import concourse.mybir as mybir
from concourse.bass_utils import run_bass_kernel_spmd
from contextlib import ExitStack

F32 = mybir.dt.float32
BF16 = mybir.dt.bfloat16
I32 = mybir.dt.int32
I16 = mybir.dt.int16
AF = mybir.ActivationFunctionType
ALU = mybir.AluOpType
BF = ml_dtypes.bfloat16

D = 2048
NCORES = 8
TCORE = 4096
NT = 512
NTILES = TCORE // NT
KC = D // 128
NFF = 16384
PTN = NFF // 256
SLABS = 8
EPS = 1e-5

ENGINES = ("sync", "scalar", "vector", "tensor", "gpsimd")


class Prog:

    def __init__(self, nc, ctx):
        self.nc = nc
        self.ctx = ctx
        self.code = {e: [] for e in ENGINES}
        self.sems = {}
        self.cnt = {}

    def sem(self, *names):
        for n in names:
            if n in self.sems:
                continue
            self.sems[n] = self.ctx.enter_context(self.nc.semaphore(n))
            self.cnt[n] = 0

    def fence(self):
        toks = [(n, self.cnt[n]) for n in ("pe", "dve", "act") if n in self.cnt]
        for q in ("sync", "scalar", "gpsimd"):
            for k in range(self.QK):
                n = f"q{q}{k}"
                if n in self.sems and self.cnt[n] > 0:
                    toks.append((n, self.cnt[n]))
        for eng in ENGINES:
            for t in toks:
                self.wait(eng, t)

    def emit(self, eng, fn):
        self.code[eng].append(fn)

    def tok(self, name):
        return (name, self.cnt[name])

    def wait(self, eng, tok):
        if tok is None or tok[1] <= 0:
            return
        n, v = tok
        self.emit(eng, lambda e, n=n, v=v: e.wait_ge(self.sems[n], v))

    def op(self, eng, fn, sem=None, k=1):
        if sem is None:
            self.emit(eng, fn)
            return None
        self.cnt[sem] += k
        v = self.cnt[sem]
        self.emit(eng, lambda e, fn=fn, n=sem, k=k: fn(e).then_inc(self.sems[n], k))
        return (sem, v)

    QK = 8

    def dma(self, q, out, in_, sem=None):
        if sem is None:
            key = ("qrr", q)
            k = self.cnt.get(key, 0)
            self.cnt[key] = (k + 1) % self.QK
            sem = f"q{q}{k}"
            if sem not in self.sems:
                self.sem(sem)
            cur = self.cnt[sem]
            if cur > 0:
                self.emit(q, lambda e, n=sem, v=cur: e.wait_ge(self.sems[n], v))
        return self.op(q, lambda e, o=out, i=in_: e.dma_start(out=o, in_=i), sem, 16)

    def qwait_all(self, eng, q):
        for k in range(self.QK):
            sem = f"q{q}{k}"
            if sem in self.sems and self.cnt[sem] > 0:
                self.wait(eng, (sem, self.cnt[sem]))

    def build(self):
        with self.nc.Block() as block:
            for eng in ENGINES:
                code = self.code[eng]
                if not code:
                    continue
                getattr(block, eng)(lambda e, code=code: [fn(e) for fn in code])


_NAME_IDX = [0]


def _uniq(name):
    _NAME_IDX[0] += 1
    return f"{name}_u{_NAME_IDX[0]}"


def _sb(nc, ctx, name, shape, dtype):
    return ctx.enter_context(nc.sbuf_tensor(_uniq(name), shape, dtype))


def _ps(nc, ctx, name, shape, dtype=F32):
    return ctx.enter_context(nc.psum_tensor(_uniq(name), shape, dtype))


def emit_dequant_matrix(p, s, pT, scT, n_jchunks, n_cols, panel, wr_fn):
    nc = p.nc
    n_panels = n_cols // panel
    units = [(c, np_) for c in range(n_jchunks) for np_ in range(n_panels)]
    hist = s.setdefault("dq_hist", [])
    for (c, np_) in units:
        gi = len(hist)
        b = gi % 2
        if gi >= 2:
            p.wait("sync", hist[gi - 2][0])
            p.wait("scalar", hist[gi - 2][0])
        t1 = p.dma("sync", s["dq_pk"][b][:],
                   pT.ap()[c * 128:(c + 1) * 128, np_ * panel:(np_ + 1) * panel])
        t2 = p.dma("scalar", s["dq_sc"][b][:],
                   scT.ap()[c * 8:(c + 1) * 8, np_ * panel:(np_ + 1) * panel]
                   .unsqueeze(1).broadcast_to([8, 16, panel]))
        if gi >= 2:
            p.wait("vector", hist[gi - 2][1])
        p.wait("vector", t1)
        p.wait("vector", t2)
        V = lambda fn: p.op("vector", fn)
        pk, sc = s["dq_pk"][b], s["dq_sc"][b]
        p16, lo, hi = s["dq_p16"], s["dq_lo"], s["dq_hi"]
        s8, mi, mf = s["dq_s8"], s["dq_mi"], s["dq_mf"]
        u1, u2, u3, sgn, ssc, sc16 = (s["dq_u1"], s["dq_u2"], s["dq_u3"],
                                      s["dq_sgn"], s["dq_ssc"], s["dq_sc16"])
        V(lambda e, pk=pk: e.tensor_copy(p16[:], pk[:]))
        V(lambda e: e.tensor_single_scalar(lo[:], p16[:], 15, ALU.bitwise_and))
        V(lambda e: e.tensor_single_scalar(hi[:], p16[:], 4, ALU.logical_shift_right))
        V(lambda e, sc=sc: e.tensor_copy(sc16[:], sc[:]))
        wr = []
        for par, u16 in ((0, lo), (1, hi)):
            wout = s["dq_w"][b][par]
            V(lambda e, u16=u16: e.tensor_single_scalar(s8[:], u16[:], 3, ALU.logical_shift_right))
            V(lambda e, u16=u16: e.scalar_tensor_tensor(mi[:], s8[:], -8, u16[:],
                                                        ALU.mult, ALU.add))
            V(lambda e: e.tensor_copy(mf[:], mi[:]))
            V(lambda e: e.tensor_scalar(u1[:], mf[:], 2.0, -8.0, ALU.mult, ALU.add))
            V(lambda e: e.scalar_tensor_tensor(u2[:], mf[:], -2.0, u1[:], ALU.add, ALU.max))
            V(lambda e: e.scalar_tensor_tensor(u3[:], mf[:], 0.5, u2[:], ALU.mult, ALU.max))
            V(lambda e: e.tensor_scalar(sgn[:], s8[:], -2.0, 1.0, ALU.mult, ALU.add))
            V(lambda e: e.tensor_tensor(ssc[:], sgn[:], sc16[:], ALU.mult))
            tok = p.op("vector",
                       lambda e, wout=wout: e.tensor_tensor(wout[:], u3[:], ssc[:], ALU.mult),
                       "dve")
            wr.append((par, wout, tok))
        wts = []
        for par, wout, tok in wr:
            p.wait("sync", tok)
            dst, srcv = wr_fn(c, par, np_, wout)
            wts.append(p.dma("sync", dst, srcv))
        hist.append((wr[-1][2], wts[-1]))
    return None


def emit_ln_stats(p, s, src_tensor, nchunks, psA, psB, rdy_tok):
    p.wait("vector", rdy_tok)
    g = s.get("stats_free")
    if g is not None:
        p.wait("vector", g)
    last = None
    for kc in range(nchunks):
        last = p.op("vector", lambda e, kc=kc: e.tensor_copy(
            s["x16f"][:, kc, :], src_tensor[:, kc, :]),
            "dve" if kc == nchunks - 1 else None)
    p.wait("tensor", last)
    tokA = None
    for kc in range(nchunks):
        tokA = p.op("tensor", lambda e, kc=kc: e.matmul(
            psA[:], s["ones128"][:, :], s["x16f"][:, kc, :],
            start=(kc == 0), stop=(kc == nchunks - 1)),
            "pe" if kc == nchunks - 1 else None)
    p.wait("vector", tokA)
    last = None
    for kc in range(nchunks):
        last = p.op("vector", lambda e, kc=kc: e.tensor_tensor(
            s["x16f"][:, kc, :], s["x16f"][:, kc, :], s["x16f"][:, kc, :], ALU.mult),
            "dve" if kc == nchunks - 1 else None)
    p.wait("tensor", last)
    tok = None
    for kc in range(nchunks):
        tok = p.op("tensor", lambda e, kc=kc: e.matmul(
            psB[:], s["ones128"][:, :], s["x16f"][:, kc, :],
            start=(kc == 0), stop=(kc == nchunks - 1)),
            "pe" if kc == nchunks - 1 else None)
    s["stats_free"] = tok
    return tok


def emit_ln_vec(p, s, psA, psB, stats_tok, n_feat, m_vec, r_vec):
    p.wait("vector", stats_tok)
    V = lambda fn, sem=None: p.op("vector", fn, sem)
    V(lambda e: e.tensor_single_scalar(m_vec[:], psA[:], 1.0 / n_feat, ALU.mult))
    V(lambda e: e.tensor_tensor(s["v_m2"][:], m_vec[:], m_vec[:], ALU.mult))
    tvar = V(lambda e: e.scalar_tensor_tensor(s["v_m2"][:], psB[:], 1.0 / n_feat,
                                              s["v_m2"][:], ALU.mult, ALU.subtract),
             "dve")
    p.wait("scalar", tvar)
    p.op("scalar", lambda e: e.activation(s["v_m2"][:], s["v_m2"][:], AF.Ln,
                                          bias=s["c_eps"][:, :]))
    tr = p.op("scalar", lambda e: e.activation(r_vec[:], s["v_m2"][:], AF.Exp,
                                               scale=-0.5), "act")
    return tr


def emit_bcast2(p, s, m_vec, r_vec, r_tok, pbm, pbr, guard=None):
    V = lambda fn, sem=None: p.op("vector", fn, sem)
    p.wait("vector", r_tok)
    toks = []
    for vec, hi, lo32, lo in ((m_vec, s["v_mh"], s["v_l32"], s["v_ml"]),
                              (r_vec, s["v_rh"], s["v_l32"], s["v_rl"])):
        V(lambda e, vec=vec, hi=hi: e.tensor_copy(hi[:], vec[:]))
        V(lambda e, hi=hi, lo32=lo32: e.tensor_copy(lo32[:], hi[:]))
        t = V(lambda e, vec=vec, lo32=lo32, lo=lo: e.tensor_tensor(
            lo[:], vec[:], lo32[:], ALU.subtract), "dve")
        toks.append(t)
    if guard is not None:
        p.wait("tensor", guard)
    p.wait("tensor", toks[0])
    p.op("tensor", lambda e: e.matmul(pbm[:], s["ones1"][:, :], s["v_mh"][:],
                                      start=True, stop=False))
    p.op("tensor", lambda e: e.matmul(pbm[:], s["ones1"][:, :], s["v_ml"][:],
                                      start=False, stop=True))
    p.wait("tensor", toks[1])
    p.op("tensor", lambda e: e.matmul(pbr[:], s["ones1"][:, :], s["v_rh"][:],
                                      start=True, stop=False))
    tok = p.op("tensor", lambda e: e.matmul(pbr[:], s["ones1"][:, :], s["v_rl"][:],
                                            start=False, stop=True), "pe")
    p.wait("scalar", tok)
    p.op("scalar", lambda e: e.copy(s["m_b"][:], pbm[:]))
    tok2 = p.op("scalar", lambda e: e.copy(s["r_b"][:], pbr[:]), "act")
    return tok2


def build_ffn_program(debug=False, ntiles=NTILES, fused=None):
    if fused is None:
        nc = bass.Bass("TRN2", target_bir_lowering=False,
                       dynamic_dma_scratch_size=8192,
                       detect_race_conditions=False)
        xT = nc.dram_tensor("xT", [D, TCORE], F32, kind="ExternalInput")
    else:
        nc, _fp, x1T = fused
        mgate = nc.dram_tensor("mgate", [128, 2], F32, kind="ExternalInput")
    g_pT = nc.dram_tensor("g_pT", [D // 2, NFF], I32, kind="ExternalInput")
    u_pT = nc.dram_tensor("u_pT", [D // 2, NFF], I32, kind="ExternalInput")
    d_pT = nc.dram_tensor("d_pT", [NFF // 2, D], I32, kind="ExternalInput")
    g_scT = nc.dram_tensor("g_scT", [D // 32, NFF], F32, kind="ExternalInput")
    u_scT = nc.dram_tensor("u_scT", [D // 32, NFF], F32, kind="ExternalInput")
    d_scT = nc.dram_tensor("d_scT", [NFF // 32, D], F32, kind="ExternalInput")
    yT = nc.dram_tensor("yT", [D, TCORE], F32, kind="ExternalOutput")
    if debug:
        dbg_wg = nc.dram_tensor("dbg_wg", [KC, 128, 256], BF16, kind="ExternalOutput")
        dbg_wd = nc.dram_tensor("dbg_wd", [16, 128, 128], BF16, kind="ExternalOutput")
        dbg_wd7 = nc.dram_tensor("dbg_wd7", [16, 128, 16, 128], BF16, kind="ExternalOutput")
        dbg_xn = nc.dram_tensor("dbg_xn", [128, KC, NT], BF16, kind="ExternalOutput")
        dbg_xp = nc.dram_tensor("dbg_xp", [2, 128, KC // 2, NT], BF16, kind="ExternalOutput")
        dbg_inter = nc.dram_tensor("dbg_inter", [2, 128, 16, NT], BF16, kind="ExternalOutput")
        dbg_acc = nc.dram_tensor("dbg_acc", [128, KC, NT], F32, kind="ExternalOutput")
        dbg_mr = nc.dram_tensor("dbg_mr", [2, 128, NT], F32, kind="ExternalOutput")
        dbg_accs = nc.dram_tensor("dbg_accs", [SLABS, 128, KC, NT], F32, kind="ExternalOutput")
    wg = nc.dram_tensor("wg", [PTN, 128, KC, 256], BF16, kind="Internal")
    wu = nc.dram_tensor("wu", [PTN, 128, KC, 256], BF16, kind="Internal")
    wd = nc.dram_tensor("wd", [SLABS, 16, 128, 16, 128], BF16, kind="Internal")

    ctx = ExitStack()
    with ctx:
        p = Prog(nc, ctx) if fused is None else _fp
        p.sem("pe", "dve", "act")
        s = {}
        if fused is not None:
            s["xbs"] = [_sb(nc, ctx, f"xbs{i}", [128, NT], F32) for i in range(2)]
            s["mgv"] = _sb(nc, ctx, "mgv", [128, 2], F32)
        s["ones1"] = _sb(nc, ctx, "ones1", [1, 128], BF16)
        s["c_eps"] = _sb(nc, ctx, "c_eps", [1, 1], F32)
        s["x"] = _sb(nc, ctx, "x", [128, KC, NT], F32)
        s["xp"] = [_sb(nc, ctx, f"xp{par}", [128, KC // 2, NT], BF16) for par in range(2)]
        s["x16f"] = _sb(nc, ctx, "x16f", [128, KC, NT], BF16)
        s["ones128"] = _sb(nc, ctx, "ones128", [128, 1], BF16)
        s["m_b"] = _sb(nc, ctx, "m_b", [128, NT], F32)
        s["r_b"] = _sb(nc, ctx, "r_b", [128, NT], F32)
        for n in ("v_m", "v_r", "v_m2", "v_l32"):
            s[n] = _sb(nc, ctx, n, [1, NT], F32)
        for n in ("v_mh", "v_ml", "v_rh", "v_rl"):
            s[n] = _sb(nc, ctx, n, [1, NT], BF16)
        s["wgbuf"] = [_sb(nc, ctx, f"wgbuf{i}", [128, KC, 256], BF16) for i in range(2)]
        s["wubuf"] = [_sb(nc, ctx, f"wubuf{i}", [128, KC, 256], BF16) for i in range(2)]
        s["wdbuf"] = [_sb(nc, ctx, f"wdbuf{i}", [128, 16, 128], BF16) for i in range(2)]
        s["inter"] = _sb(nc, ctx, "inter", [128, 16, NT], BF16)
        s["acc"] = _sb(nc, ctx, "acc", [128, 16, NT], F32)
        for n in ("ge_sg", "ge_su", "ge_pb", "ge_th"):
            s[n] = _sb(nc, ctx, n, [128, NT], F32)
        s["out2"] = [_sb(nc, ctx, f"out2{i}", [128, NT], F32) for i in range(2)]
        s["t_ln"] = _sb(nc, ctx, "t_ln", [128, NT], F32)
        pp = [_ps(nc, ctx, f"pp{i}", [128, NT], F32) for i in range(8)]

        p.op("gpsimd", lambda e: e.memset(s["ones128"][:], 1.0))
        p.op("gpsimd", lambda e: e.memset(s["ones1"][:], 1.0))
        cinit = p.op("gpsimd", lambda e: e.memset(s["c_eps"][:], EPS), "act", 1)
        if fused is not None:
            mg_tok = p.dma("gpsimd", s["mgv"][:], mgate.ap()[:, :])

        def wr_gate(which):
            def fn(c, par, np_, wout):
                nb = 512 // 256
                pt0 = np_ * nb
                dst = which.ap()[pt0:pt0 + nb, :, 2 * c + par, :].transpose([1, 0, 2])
                return dst, wout[:].rearrange("p (b n) -> p b n", b=nb)
            return fn

        def wr_down(c, par, np_, wout):
            kcp = 2 * c + par
            nb = 512 // 128
            ht0 = np_ * nb
            dst = wd.ap()[kcp // 16, ht0:ht0 + nb, :, kcp % 16, :].transpose([1, 0, 2])
            return dst, wout[:].rearrange("p (b n) -> p b n", b=nb)

        PAN = 512
        dqctx = ExitStack()
        with dqctx:
            s["dq_pk"] = [_sb(nc, dqctx, f"dq_pk{i}", [128, PAN], I32) for i in range(2)]
            s["dq_sc"] = [_sb(nc, dqctx, f"dq_sc{i}", [128, PAN], F32) for i in range(2)]
            for n, d in (("dq_p16", I16), ("dq_lo", I16), ("dq_hi", I16), ("dq_s8", I16),
                         ("dq_mi", I16), ("dq_mf", BF16), ("dq_u1", BF16), ("dq_u2", BF16),
                         ("dq_u3", BF16), ("dq_sgn", BF16), ("dq_ssc", BF16),
                         ("dq_sc16", BF16)):
                s[n] = _sb(nc, dqctx, n, [128, PAN], d)
            s["dq_w"] = [[_sb(nc, dqctx, f"dq_w{i}{par}", [128, PAN], BF16)
                          for par in range(2)] for i in range(2)]
            emit_dequant_matrix(p, s, g_pT, g_scT, 8, NFF, PAN, wr_gate(wg))
            emit_dequant_matrix(p, s, u_pT, u_scT, 8, NFF, PAN, wr_gate(wu))
            emit_dequant_matrix(p, s, d_pT, d_scT, 64, D, PAN, wr_down)
        dq_done = True

        if debug:
            p.qwait_all("gpsimd", "sync")
            p.dma("gpsimd", dbg_wg[:, :, :], wg.ap()[0, :, :, :].transpose([1, 0, 2]))
            p.dma("gpsimd", dbg_wd[:, :, :], wd.ap()[0, :, :, 0, :])
            p.dma("gpsimd", dbg_wd7.ap(), wd.ap()[7])

        wg_free = [None, None]
        wu_free = [None, None]
        wd_free = [None, None]
        x_free = None
        out_free = [None, None]
        for ti in range(ntiles):
            t0 = ti * NT
            if x_free is not None:
                p.wait("scalar", x_free)
            if fused is None:
                x_tok = p.dma("scalar", s["x"][:],
                              xT.ap().rearrange("(c p) t -> p c t", p=128)
                              [:, :, t0:t0 + NT])
            else:
                x1r = x1T.ap().rearrange("(c p) t -> p c t", p=128)
                hoff = x1T.shape[1] // 2
                xa_t = p.dma("scalar", s["x"][:], x1r[:, :, t0:t0 + NT])
                p.wait("vector", xa_t)
                if ti == 0:
                    p.wait("vector", mg_tok)
                x_tok = None
                for kc in range(KC):
                    b = kc % 2
                    xbg = s.get(("xbs_free", b))
                    if xbg is not None:
                        p.wait("scalar", xbg)
                    xb_t = p.dma("scalar", s["xbs"][b][:],
                                 x1r[:, kc, hoff + t0:hoff + t0 + NT])
                    p.wait("vector", xb_t)
                    p.op("vector", lambda e, kc=kc: e.tensor_scalar_mul(
                        s["x"][:, kc, :], s["x"][:, kc, :], s["mgv"][:, 0:1]))
                    x_tok = p.op("vector", lambda e, kc=kc, b=b:
                                 e.scalar_tensor_tensor(
                                     s["x"][:, kc, :], s["xbs"][b][:],
                                     s["mgv"][:, 1:2], s["x"][:, kc, :],
                                     ALU.mult, ALU.add), "dve")
                    s[("xbs_free", b)] = x_tok
            for bk in range(4):
                g = s.get(("bankfree", bk))
                if g is not None:
                    p.wait("tensor", g)
            st_tok = emit_ln_stats(p, s, s["x"], KC, pp[0][0:1, :], pp[1][0:1, :],
                                   x_tok)
            r_tok = emit_ln_vec(p, s, pp[0][0:1, :], pp[1][0:1, :], st_tok, D,
                                s["v_m"], s["v_r"])
            mb_tok = emit_bcast2(p, s, s["v_m"], s["v_r"], r_tok, pp[2], pp[3])
            p.wait("vector", mb_tok)
            for kc in range(KC):
                p.op("vector", lambda e, kc=kc: e.tensor_tensor(
                    s["t_ln"][:], s["x"][:, kc, :], s["m_b"][:], ALU.subtract))
                xn_tok = p.op("vector", lambda e, kc=kc: e.tensor_tensor(
                    s["x16f"][:, kc, :], s["t_ln"][:], s["r_b"][:], ALU.mult), "dve")
            p.wait("sync", xn_tok)
            xpg = s.get("xp_free")
            if xpg is not None:
                p.wait("sync", xpg)
            xp_toks = []
            xnv = s["x16f"]
            for par in range(2):
                for half in range(2):
                    dst = s["xp"][par][64 * half:64 * half + 64, :, :]
                    srcv = xnv[par:128:2, half:KC:2, :]
                    xp_toks.append(p.dma("sync", dst, srcv))
            s["stats_free"] = xp_toks[-1]
            xp_rdy = xp_toks[-1]
            if ti == 0:
                p.qwait_all("scalar", "sync")
            if debug and ti == 0:
                p.wait("gpsimd", xp_rdy)
                p.dma("gpsimd", dbg_xn[:, :, :], s["x16f"][:])
                p.dma("gpsimd", dbg_xp[0], s["xp"][0][:])
                p.dma("gpsimd", dbg_xp[1], s["xp"][1][:])
                p.dma("gpsimd", dbg_mr[0], s["m_b"][:])
                p.dma("gpsimd", dbg_mr[1], s["r_b"][:])

            for sl in range(SLABS):
                it_toks = []
                for j in range(8):
                    pt = sl * 8 + j
                    bw = pt % 2
                    if ti == 0 and sl == 0 and j < 2:
                        p.qwait_all("sync", "sync")
                    if wg_free[bw] is not None:
                        p.wait("sync", wg_free[bw])
                    gld = p.dma("sync", s["wgbuf"][bw][:], wg.ap()[pt, :, :, :])
                    if wu_free[bw] is not None:
                        p.wait("sync", wu_free[bw])
                    uld = p.dma("sync", s["wubuf"][bw][:], wu.ap()[pt, :, :, :])
                    bank = (j % 2) * 4
                    p.wait("tensor", gld)
                    p.wait("tensor", uld)
                    if sl == 0 and j == 0:
                        p.wait("tensor", xp_rdy)
                    grp = []
                    for gi, (wbuf, colsl) in enumerate((
                            (s["wgbuf"][bw], slice(0, 256, 2)),
                            (s["wgbuf"][bw], slice(1, 256, 2)),
                            (s["wubuf"][bw], slice(0, 256, 2)),
                            (s["wubuf"][bw], slice(1, 256, 2)))):
                        pb = pp[bank + gi]
                        g = s.get(("bankfree", bank + gi))
                        if g is not None:
                            p.wait("tensor", g)
                        tok = None
                        for kcp in range(KC):
                            par, cc = kcp % 2, kcp // 2
                            tok = p.op("tensor", lambda e, wbuf=wbuf, colsl=colsl,
                                       kcp=kcp, par=par, cc=cc, pb=pb: e.matmul(
                                           pb[:], wbuf[:, kcp, colsl],
                                           s["xp"][par][:, cc, :],
                                           start=(kcp == 0), stop=(kcp == KC - 1)),
                                       "pe" if kcp == KC - 1 else None)
                        grp.append(tok)
                    wg_free[bw] = grp[1]
                    wu_free[bw] = grp[3]
                    s["xp_free"] = grp[3]
                    pg_ev, pg_od = pp[bank + 0], pp[bank + 1]
                    pu_ev, pu_od = pp[bank + 2], pp[bank + 3]
                    for par, pg, pu in ((0, pg_ev, pu_ev), (1, pg_od, pu_od)):
                        gidx = bank + (0 if par == 0 else 1)
                        uidx = bank + (2 if par == 0 else 3)
                        p.wait("scalar", grp[2] if par == 0 else grp[3])
                        lg = s.get("last_gelu")
                        if lg is not None:
                            p.wait("scalar", lg)
                        p.op("scalar", lambda e, pg=pg: e.copy(s["ge_sg"][:], pg[:]))
                        tsu = p.op("scalar", lambda e, pu=pu: e.copy(s["ge_su"][:], pu[:]),
                                   "act")
                        s[("bankfree", gidx)] = tsu
                        s[("bankfree", uidx)] = tsu
                        V = lambda fn, sem=None: p.op("vector", fn, sem)
                        p.wait("vector", tsu)
                        V(lambda e: e.tensor_tensor(s["ge_pb"][:], s["ge_sg"][:],
                                                    s["ge_sg"][:], ALU.mult))
                        V(lambda e: e.tensor_scalar(s["ge_pb"][:], s["ge_pb"][:],
                                                    0.044715, 1.0, ALU.mult, ALU.add))
                        tpc = V(lambda e: e.tensor_tensor(s["ge_pb"][:], s["ge_sg"][:],
                                                          s["ge_pb"][:], ALU.mult), "dve")
                        p.wait("scalar", tpc)
                        tth = p.op("scalar", lambda e: e.activation(
                            s["ge_th"][:], s["ge_pb"][:], AF.Tanh, scale=0.7978845608028654),
                            "act")
                        V(lambda e: e.tensor_tensor(s["ge_sg"][:], s["ge_sg"][:],
                                                    s["ge_su"][:], ALU.mult))
                        p.wait("vector", tth)
                        V(lambda e: e.tensor_scalar(s["ge_th"][:], s["ge_th"][:],
                                                    0.5, 0.5, ALU.mult, ALU.add))
                        w = 2 * j + par
                        it = V(lambda e, w=w: e.tensor_tensor(
                            s["inter"][:, w, :], s["ge_th"][:], s["ge_sg"][:], ALU.mult),
                            "dve")
                        s["last_gelu"] = it
                        it_toks.append(it)
                inter_rdy = it_toks[-1]
                if debug and ti == 0 and sl in (0, 7):
                    p.wait("gpsimd", inter_rdy)
                    p.dma("gpsimd", dbg_inter.ap()[0 if sl == 0 else 1], s["inter"][:])
                for ht in range(16):
                    bw = ht % 2
                    if wd_free[bw] is not None:
                        p.wait("scalar", wd_free[bw])
                    dld = p.dma("scalar", s["wdbuf"][bw][:],
                                wd.ap()[sl, ht, :, :, :])
                    bank = ht % 4
                    g = s.get(("bankfree", bank))
                    if g is not None:
                        p.wait("tensor", g)
                    p.wait("tensor", dld)
                    p.wait("tensor", inter_rdy)
                    mm_tok = None
                    for kcw in range(16):
                        mm_tok = p.op("tensor", lambda e, bw=bw, kcw=kcw, bank=bank:
                                      e.matmul(pp[bank][:], s["wdbuf"][bw][:, kcw, :],
                                               s["inter"][:, kcw, :],
                                               start=(kcw == 0), stop=(kcw == 15)),
                                      "pe" if kcw == 15 else None)
                    wd_free[bw] = mm_tok
                    p.wait("vector", mm_tok)
                    if sl == 0:
                        dt = p.op("vector", lambda e, ht=ht, bank=bank: e.tensor_copy(
                            s["acc"][:, ht, :], pp[bank][:]), "dve")
                    else:
                        dt = p.op("vector", lambda e, ht=ht, bank=bank: e.tensor_tensor(
                            s["acc"][:, ht, :], s["acc"][:, ht, :], pp[bank][:], ALU.add),
                            "dve")
                    s[("bankfree", bank)] = dt
                if debug and ti == 0:
                    p.wait("gpsimd", p.tok("dve"))
                    dtk = p.dma("gpsimd", dbg_accs.ap()[sl], s["acc"][:])
                    p.wait("vector", dtk)
            acc_done = p.tok("dve")
            if debug and ti == 0:
                p.wait("gpsimd", acc_done)
                p.dma("gpsimd", dbg_acc[:, :, :], s["acc"][:])

            for bk in range(4, 8):
                g = s.get(("bankfree", bk))
                if g is not None:
                    p.wait("tensor", g)
            st4 = emit_ln_stats(p, s, s["acc"], KC, pp[4][0:1, :], pp[5][0:1, :],
                                acc_done)
            r4 = emit_ln_vec(p, s, pp[4][0:1, :], pp[5][0:1, :], st4, D,
                             s["v_m"], s["v_r"])
            mb4 = emit_bcast2(p, s, s["v_m"], s["v_r"], r4, pp[6], pp[7])
            p.wait("vector", mb4)
            for kc in range(KC):
                ob = kc % 2
                if out_free[ob] is not None:
                    p.wait("vector", out_free[ob])
                p.op("vector", lambda e, kc=kc: e.tensor_tensor(
                    s["t_ln"][:], s["acc"][:, kc, :], s["m_b"][:], ALU.subtract))
                p.op("vector", lambda e: e.tensor_tensor(
                    s["t_ln"][:], s["t_ln"][:], s["r_b"][:], ALU.mult))
                ot = p.op("vector", lambda e, kc=kc, ob=ob: e.tensor_tensor(
                    s["out2"][ob][:], s["t_ln"][:], s["x"][:, kc, :], ALU.add), "dve")
                p.wait("scalar", ot)
                out_free[ob] = p.dma(
                    "scalar", yT.ap().rearrange("(c p) t -> p c t", p=128)
                    [:, kc, t0:t0 + NT], s["out2"][ob][:])
            x_free = out_free[(KC - 1) % 2]

        p.qwait_all("gpsimd", "scalar")
        p.qwait_all("gpsimd", "sync")
        p.qwait_all("gpsimd", "gpsimd")
        if fused is None:
            p.build()
    return nc


def ffn_in_maps(x1, g_p, g_s, u_p, u_s, d_p, d_s):
    xt = np.ascontiguousarray(x1.reshape(NCORES, TCORE, D).transpose(0, 2, 1))
    g_pT = np.ascontiguousarray(g_p.T)
    u_pT = np.ascontiguousarray(u_p.T)
    d_pT = np.ascontiguousarray(d_p.T)
    g_scT = np.ascontiguousarray(g_s.T)
    u_scT = np.ascontiguousarray(u_s.T)
    d_scT = np.ascontiguousarray(d_s.T)
    return [dict(xT=xt[c], g_pT=g_pT, u_pT=u_pT, d_pT=d_pT,
                 g_scT=g_scT, u_scT=u_scT, d_scT=d_scT) for c in range(NCORES)]


def ffn_gather(results):
    out = np.stack([r["yT"].T for r in results], axis=0)
    return out.reshape(4, 8192, D)


def build_fused_program():
    nc = bass.Bass("TRN2", target_bir_lowering=False, dynamic_dma_scratch_size=8192,
                   detect_race_conditions=False)
    octx = ExitStack()
    with octx:
        p = Prog(nc, octx)
        x1T = build_mamba_program(ntiles=16, fused=(nc, p))
        p.fence()
        build_ffn_program(ntiles=NTILES, fused=(nc, p, x1T))
        p.build()
    return nc


def fused_in_maps(ins):
    maps = mamba_in_maps(ins)
    g_pT = np.ascontiguousarray(np.asarray(ins["L0_ffn_gate_W_packed"]).T)
    u_pT = np.ascontiguousarray(np.asarray(ins["L0_ffn_up_W_packed"]).T)
    d_pT = np.ascontiguousarray(np.asarray(ins["L0_ffn_down_W_packed"]).T)
    g_scT = np.ascontiguousarray(np.asarray(ins["L0_ffn_gate_scales"]).T)
    u_scT = np.ascontiguousarray(np.asarray(ins["L0_ffn_up_scales"]).T)
    d_scT = np.ascontiguousarray(np.asarray(ins["L0_ffn_down_scales"]).T)
    for c in range(NCORES):
        half = c // 4
        mg = np.zeros((128, 2), np.float32)
        mg[:, half] = 1.0
        maps[c].update(dict(g_pT=g_pT, u_pT=u_pT, d_pT=d_pT, g_scT=g_scT,
                            u_scT=u_scT, d_scT=d_scT, mgate=mg))
    return maps


def kernel(**inputs):
    nc = build_fused_program()
    res = run_bass_kernel_spmd(nc, fused_in_maps(inputs),
                               core_ids=list(range(NCORES)))
    out = np.empty((4, 8192, D), np.float32)
    for c in range(NCORES):
        seq, half = c % 4, c // 4
        out[seq, half * TCORE:(half + 1) * TCORE] = np.asarray(
            res.results[c]["yT"], np.float32).T
    return out



ZCT = 32
XCT = 34
DTCT = 66
NCT = 67


def build_mamba_program(debug=False, ntiles=16, trunc=99, nocast=False, fused=None):
    TOK = ntiles * NT
    if fused is None:
        nc = bass.Bass("TRN2", target_bir_lowering=False,
                       dynamic_dma_scratch_size=8192,
                       detect_race_conditions=False)
    else:
        nc, _fp = fused
    uT = nc.dram_tensor("uT", [D, TOK], F32, kind="ExternalInput")
    wi = nc.dram_tensor("wi", [D, 8512], F32, kind="ExternalInput")
    wo = nc.dram_tensor("wo", [4096, D], F32, kind="ExternalInput")
    cw = nc.dram_tensor("cw", [4352, 4], F32, kind="ExternalInput")
    cbv = nc.dram_tensor("cbv", [128, XCT], F32, kind="ExternalInput")
    dtb = nc.dram_tensor("dtb", [64], F32, kind="ExternalInput")
    alog = nc.dram_tensor("alog", [64], F32, kind="ExternalInput")
    dvec = nc.dram_tensor("dvec", [64], F32, kind="ExternalInput")
    nwv = nc.dram_tensor("nwv", [128, 32], F32, kind="ExternalInput")
    x1T = nc.dram_tensor("x1T", [D, TOK], F32,
                         kind="Internal" if fused is not None else "ExternalOutput")
    wi_blk = nc.dram_tensor("wi_blk", [NCT, 128, KC, 128], BF16, kind="Internal")
    wo_blk = nc.dram_tensor("wo_blk", [16, 128, 32, 128], BF16, kind="Internal")
    rdram = nc.dram_tensor("rdram", [2, 64, 256], F32, kind="Internal")
    cddram = nc.dram_tensor("cddram", [64], F32, kind="Internal")
    if debug:
        d_xn = nc.dram_tensor("d_xn", [128, KC, NT], BF16, kind="ExternalOutput")
        d_xpost = nc.dram_tensor("d_xpost", [128, XCT, 256], BF16, kind="ExternalOutput")
        d_dts = nc.dram_tensor("d_dts", [64, 256], F32, kind="ExternalOutput")
        d_acs = nc.dram_tensor("d_acs", [64, 256], F32, kind="ExternalOutput")
        d_cbtm = nc.dram_tensor("d_cbtm", [128, 2, 256], BF16, kind="ExternalOutput")
        d_zsil = nc.dram_tensor("d_zsil", [128, 2, NT], F32, kind="ExternalOutput")
        d_x = nc.dram_tensor("d_x", [128, KC, NT], F32, kind="ExternalOutput")
        d_Mb0 = nc.dram_tensor("d_Mb0", [128, 2, 256], BF16, kind="ExternalOutput")
        d_xdt0 = nc.dram_tensor("d_xdt0", [128, 2, 64], BF16, kind="ExternalOutput")
        d_xt80 = nc.dram_tensor("d_xt80", [128, 2, 2, 128], BF16, kind="ExternalOutput")
        d_dtt0 = nc.dram_tensor("d_dtt0", [128, 2, 64], F32, kind="ExternalOutput")
        d_ybuf = nc.dram_tensor("d_ybuf", [128, 2, 4096], BF16, kind="ExternalOutput")
        d_ygate = nc.dram_tensor("d_ygate", [128, 2, 4096], BF16, kind="ExternalOutput")
        d_ynt = nc.dram_tensor("d_ynt", [128, 32, 256], BF16, kind="ExternalOutput")
        d_obuf = nc.dram_tensor("d_obuf", [128, KC, NT], BF16, kind="ExternalOutput")

    eye16_np = np.eye(128, dtype=ml_dtypes.bfloat16)
    eyef_np = np.eye(128, dtype=np.float32)
    tri_np = np.zeros((128, 2, 256), dtype=ml_dtypes.bfloat16)
    for st in range(2):
        for ss in range(128):
            tri_np[ss, st, ss + 128 * st:] = 1
    eye16_c = nc.inline_tensor(eye16_np, name=_uniq("eye16_c"))
    eyef_c = nc.inline_tensor(eyef_np, name=_uniq("eyef_c"))
    tri_c = nc.inline_tensor(tri_np, name=_uniq("tri_c"))

    ctx = ExitStack()
    with ctx:
        p = Prog(nc, ctx) if fused is None else _fp
        p.sem("pe", "dve", "act")
        s = {}
        sb = lambda n, shp, d: _sb(nc, ctx, n, shp, d)
        s["x16f"] = sb("x16f", [128, KC, NT], BF16)
        s["ones128"] = sb("ones128", [128, 1], BF16)
        s["ones1"] = sb("ones1", [1, 128], BF16)
        s["c_eps"] = sb("c_eps", [1, 1], F32)
        s["m_b"] = sb("m_b", [128, NT], F32)
        s["r_b"] = sb("r_b", [128, NT], F32)
        for n in ("v_m", "v_r", "v_m2", "v_l32"):
            s[n] = sb(n, [1, NT], F32)
        for n in ("v_mh", "v_ml", "v_rh", "v_rl"):
            s[n] = sb(n, [1, NT], BF16)
        s["t_ln"] = sb("t_ln", [128, NT], F32)
        s["out2"] = [sb(f"out2{i}", [128, NT], F32) for i in range(2)]
        x = sb("x", [128, KC, NT], F32)
        xpost = sb("xpost", [128, XCT, 256], BF16)
        halo = sb("halo", [128, XCT, 3], F32)
        convin = sb("convin", [128, 260], F32)
        cacc = sb("cacc", [128, 256], F32)
        cwt = sb("cwt", [128, XCT, 4], F32)
        cbt = sb("cbt", [128, XCT], F32)
        dtv = sb("dtv", [64, 256], F32)
        dts = sb("dts", [64, 256], F32)
        dsc1 = sb("dsc1", [64, 256], F32)
        dsc2 = sb("dsc2", [64, 256], F32)
        dAv = sb("dAv", [64, 256], F32)
        zer64 = sb("zer64", [64, 256], F32)
        acs = sb("acs", [64, 256], F32)
        decf = sb("decf", [64, 256], F32)
        sdf = sb("sdf", [64, 256], F32)
        cdcol = sb("cdcol", [64, 1], F32)
        acst = sb("acst", [128, 2, 64], F32)
        nacst = sb("nacst", [128, 2, 64], F32)
        dtt = sb("dtt", [128, 2, 64], F32)
        dect = sb("dect", [128, 2, 64], F32)
        bcband = sb("bcband", [128, 4, 256], F32)
        sdband = sb("sdband", [128, 4, 256], F32)
        cbtm = sb("cbtm", [128, 2, 256], BF16)
        csd = [sb(f"csd{i}", [128, 256], BF16) for i in range(2)]
        Mb = [sb(f"Mb{i}", [128, 2, 256], BF16) for i in range(2)]
        ecl = [sb(f"ecl{i}", [128, 256], F32) for i in range(2)]
        exb = [sb(f"exb{i}", [128, 256], BF16) for i in range(2)]
        xdt = [sb(f"xdt{i}", [128, 2, 64], BF16) for i in range(2)]
        xdd = [sb(f"xdd{i}", [128, 2, 64], BF16) for i in range(2)]
        S16 = [sb(f"S16{i}", [128, 64], BF16) for i in range(2)]
        xt8 = sb("xt8", [128, 2, 2, 128], BF16)
        btb = sb("btb", [128, 2, 128], BF16)
        Sbuf = sb("Sbuf", [128, 4096], F32)
        ybuf = sb("ybuf", [128, 2, 4096], BF16)
        zsil = sb("zsil", [128, NT], F32)
        ynt = sb("ynt", [128, 32, 256], BF16)
        obuf = sb("obuf", [128, KC, NT], BF16)
        wibuf = [sb(f"wibuf{i}", [128, KC, 128], BF16) for i in range(2)]
        wobuf = sb("wobuf", [128, 32, 128], BF16)
        D_b = sb("D_b", [128, 64], F32)
        cdb = sb("cdb", [128, 64], F32)
        dtbcol = sb("dtbcol", [64, 1], F32)
        A_col = sb("A_col", [64, 1], F32)
        nwt = sb("nwt", [128, 32], F32)
        rsum = [sb(f"rsum{i}", [128, 1], F32) for i in range(2)]
        c_eps128 = sb("c_eps128", [128, 1], F32)
        rstd = sb("rstd", [128, 1], F32)
        eye16 = sb("eye16", [128, 128], BF16)
        eyef = sb("eyef", [128, 128], F32)
        tri = sb("tri", [128, 2, 256], BF16)
        pp = [_ps(nc, ctx, f"pp{i}", [128, NT], F32) for i in range(6)]
        ptf = _ps(nc, ctx, "ptf", [128, 128], F32)
        pt16 = _ps(nc, ctx, "pt16", [128, 128], BF16)
        bk = {}

        def bank_wait(i):
            g = bk.get(i)
            if g is not None:
                p.wait("tensor", g)

        p.op("gpsimd", lambda e: e.memset(s["ones128"][:], 1.0))
        p.op("gpsimd", lambda e: e.memset(s["ones1"][:], 1.0))
        p.op("gpsimd", lambda e: e.memset(s["c_eps"][:], EPS))
        p.op("gpsimd", lambda e: e.memset(c_eps128[:], EPS))
        p.op("gpsimd", lambda e: e.memset(zer64[:], 0.0))
        p.op("gpsimd", lambda e: e.memset(halo[:], 0.0))
        ginit = p.op("gpsimd", lambda e: e.memset(Sbuf[:], 0.0), "act")
        p.dma("gpsimd", eye16[:], eye16_c[:, :])
        p.dma("gpsimd", eyef[:], eyef_c[:, :])
        p.dma("gpsimd", tri[:], tri_c[:, :, :])
        p.dma("gpsimd", cwt[:], cw.ap().rearrange("(c p) k -> p c k", p=128))
        p.dma("gpsimd", cbt[:], cbv.ap()[:, :])
        p.dma("gpsimd", dtbcol[:], dtb.ap().unsqueeze(1))
        p.dma("gpsimd", A_col[:], alog.ap().unsqueeze(1))
        p.dma("gpsimd", D_b[:], dvec.ap().unsqueeze(0).broadcast_to([128, 64]))
        p.dma("gpsimd", nwt[:], nwv.ap()[:, :])
        cast_blocks = []
        if not nocast:
            wir = wi.ap().rearrange("(kc p) n -> p kc n", p=128)
            wor = wo.ap().rearrange("(kc p) n -> p kc n", p=128)
            for ct in range(NCT):
                w = 128 if ct < DTCT else 64
                cast_blocks.append((wir[:, :, ct * 128:ct * 128 + w],
                                    wi_blk.ap()[ct, :, :, 0:w], w))
            for dmt in range(16):
                for half in range(2):
                    cast_blocks.append(
                        (wor[:, half * 16:half * 16 + 16,
                             dmt * 128:(dmt + 1) * 128],
                         wo_blk.ap()[dmt, :, half * 16:half * 16 + 16, :], 128))
        cast_free = [None, None]
        out_free = [None, None]
        for i, (src_ap, dst_ap, w) in enumerate(cast_blocks):
            b = i % 2
            c0 = b * 128
            if cast_free[b] is not None:
                p.wait("sync", cast_free[b])
            ld = p.dma("sync", x[:, :, c0:c0 + w], src_ap)
            p.wait("vector", ld)
            if out_free[b] is not None:
                p.wait("vector", out_free[b])
            cp = p.op("vector", lambda e, c0=c0, w=w: e.tensor_copy(
                s["x16f"][:, :, c0:c0 + w], x[:, :, c0:c0 + w]), "dve")
            cast_free[b] = cp
            p.wait("scalar", cp)
            out_free[b] = p.dma("scalar", dst_ap, s["x16f"][:, :, c0:c0 + w])
        p.qwait_all("scalar", "gpsimd")
        ta = p.op("scalar", lambda e: e.activation(A_col[:], A_col[:], AF.Exp), "act")
        p.wait("vector", ta)
        av = p.op("vector", lambda e: e.tensor_scalar_mul(A_col[:], A_col[:], -1.0),
                  "dve")
        p.qwait_all("sync", "scalar")
        p.qwait_all("sync", "gpsimd")

        x_free = None
        xt8_free = None
        wib_free = [None, None]
        wob_free = None
        prev_cs = None
        last_z_mt = None
        band_free = None
        prev_bb = prev_sd = prev_cdb = None
        ydx = {}
        ze_t = None
        for ti in range(ntiles):
            t0 = ti * NT
            if x_free is not None:
                p.wait("scalar", x_free)
            if ti == 0:
                p.wait("scalar", cast_free[0])
                p.wait("scalar", cast_free[1])
            x_tok = p.dma("scalar", x[:],
                          uT.ap().rearrange("(c p) t -> p c t", p=128)[:, :, t0:t0 + NT])
            if trunc <= 0:
                if debug and ti == 0:
                    p.wait("gpsimd", x_tok)
                    p.dma("gpsimd", d_x[:, :, :], x[:])
                continue
            if ti == 0:
                p.wait("vector", out_free[0])
                p.wait("vector", out_free[1])
            for bi in range(4):
                bank_wait(bi)
            st_tok = emit_ln_stats(p, s, x, KC, pp[0][0:1, :], pp[1][0:1, :], x_tok)
            r_tok = emit_ln_vec(p, s, pp[0][0:1, :], pp[1][0:1, :], st_tok, D,
                                s["v_m"], s["v_r"])
            mb_tok = emit_bcast2(p, s, s["v_m"], s["v_r"], r_tok, pp[2], pp[3])
            bk[0] = bk[1] = bk[2] = bk[3] = mb_tok
            p.wait("vector", mb_tok)
            xn_tok = None
            for kc in range(KC):
                p.op("vector", lambda e, kc=kc: e.tensor_tensor(
                    s["t_ln"][:], x[:, kc, :], s["m_b"][:], ALU.subtract))
                xn_tok = p.op("vector", lambda e, kc=kc: e.tensor_tensor(
                    s["x16f"][:, kc, :], s["t_ln"][:], s["r_b"][:], ALU.mult), "dve")
            if debug and ti == 0:
                p.wait("gpsimd", xn_tok)
                p.dma("gpsimd", d_xn[:, :, :], s["x16f"][:])
            if trunc <= 1:
                continue

            for c in range(2):
                cs0 = c * 256
                csl = slice(cs0, cs0 + 256)
                for ict in range(XCT + 1):
                    ct = 32 + ict
                    bwi = ict % 2
                    if wib_free[bwi] is not None:
                        p.wait("sync", wib_free[bwi])
                    wld = 64 if ct == DTCT else 128
                    ld = p.dma("sync", wibuf[bwi][:, :, 0:wld],
                               wi_blk.ap()[ct, :, :, 0:wld])
                    pb = pp[ict % 2]
                    bank_wait(ict % 2)
                    p.wait("tensor", ld)
                    p.wait("tensor", xn_tok)
                    mmw = 64 if ct == DTCT else 128
                    mt = None
                    for kc in range(KC):
                        mt = p.op("tensor", lambda e, bwi=bwi, kc=kc, pb=pb, mmw=mmw,
                                  csl=csl: e.matmul(
                                      pb[0:mmw, 0:256],
                                      wibuf[bwi][:, kc, 0:mmw],
                                      s["x16f"][:, kc, csl],
                                      start=(kc == 0), stop=(kc == KC - 1)),
                                  "pe" if kc == KC - 1 else None)
                    wib_free[bwi] = mt
                    if ct == DTCT:
                        p.wait("scalar", mt)
                        dt_ev = p.op("scalar", lambda e, pb=pb: e.copy(
                            dtv[:], pb[0:64, 0:256]), "act")
                        bk[ict % 2] = dt_ev
                    else:
                        cti = ict
                        p.wait("scalar", mt)
                        if prev_cs is not None:
                            p.wait("scalar", prev_cs)
                        ev = p.op("scalar", lambda e, pb=pb: e.copy(
                            convin[:, 3:259], pb[:, 0:256]), "act")
                        bk[ict % 2] = ev
                        p.wait("vector", ev)
                        V = lambda fn, sem=None: p.op("vector", fn, sem)
                        V(lambda e, cti=cti: e.tensor_copy(convin[:, 0:3],
                                                           halo[:, cti, :]))
                        V(lambda e, cti=cti: e.tensor_scalar(
                            cacc[:], convin[:, 0:256], cwt[:, cti, 0:1],
                            cbt[:, cti:cti + 1], ALU.mult, ALU.add))
                        for k in range(1, 4):
                            V(lambda e, cti=cti, k=k: e.scalar_tensor_tensor(
                                cacc[:], convin[:, k:256 + k], cwt[:, cti, k:k + 1],
                                cacc[:], ALU.mult, ALU.add))
                        prev_cs = p.op("vector", lambda e, cti=cti: e.tensor_copy(
                            halo[:, cti, :], convin[:, 256:259]), "dve")
                        p.wait("scalar", prev_cs)
                        p.op("scalar", lambda e, cti=cti: e.activation(
                            xpost[:, cti, :], cacc[:], AF.Silu), "act")
                conv_done = p.tok("act")
                if debug and ti == 0 and c == 0:
                    p.wait("gpsimd", conv_done)
                    p.dma("gpsimd", d_xpost[:, :, :], xpost[:])
                if trunc <= 2:
                    continue

                p.op("scalar", lambda e: e.activation(dsc1[:], dtv[:], AF.Relu,
                                                      bias=dtbcol[:, :]))
                p.op("scalar", lambda e: e.activation(dsc2[:], dtv[:], AF.Abs,
                                                      bias=dtbcol[:, :]))
                p.op("scalar", lambda e: e.activation(dsc2[:], dsc2[:], AF.Exp,
                                                      scale=-1.0))
                sp_tok = p.op("scalar", lambda e: e.activation(dsc2[:], dsc2[:], AF.Ln,
                                                               bias=1.0), "act")
                p.wait("vector", sp_tok)
                if ti == 0 and c == 0:
                    p.wait("vector", av)
                p.op("vector", lambda e: e.tensor_tensor(dts[:], dsc1[:], dsc2[:],
                                                         ALU.add))
                dA_tok = p.op("vector", lambda e: e.tensor_scalar_mul(
                    dAv[:], dts[:], A_col[:, :]), "dve")
                if debug and ti == 0 and c == 0:
                    p.wait("gpsimd", dA_tok)
                    p.dma("gpsimd", d_dts[:, :], dts[:])

                p.wait("vector", dA_tok)
                sc_tok = p.op("vector", lambda e: e.tensor_tensor_scan(
                    acs[:, :], dAv[:, :], zer64[:, :], 0.0, ALU.add, ALU.add),
                    "dve")
                alast = acs[:, 255:256]
                p.wait("scalar", sc_tok)
                p.op("scalar", lambda e, alast=alast: e.activation(
                    decf[:], acs[:, :], AF.Exp, bias=alast, scale=-1.0))
                p.op("scalar", lambda e: e.activation(
                    sdf[:], acs[:, :], AF.Exp))
                cd_tok = p.op("scalar", lambda e, alast=alast: e.activation(
                    cdcol[:], alast, AF.Exp), "act")
                if debug and ti == 0 and c == 0:
                    p.wait("gpsimd", cd_tok)
                    p.dma("gpsimd", d_acs[:, 0:256], acs[:, 0:256])
                p.wait("scalar", cd_tok)
                if prev_bb is not None:
                    p.wait("scalar", prev_bb)
                    p.wait("scalar", prev_sd)
                    p.wait("scalar", prev_cdb)
                if band_free is not None:
                    p.wait("scalar", band_free)
                rd0_t = p.dma("scalar", rdram.ap()[0], acs[:, :])
                rd1_t = p.dma("scalar", rdram.ap()[1], sdf[:])
                cdd_t = p.dma("scalar", cddram.ap().unsqueeze(1), cdcol[:])
                p.wait("scalar", cdd_t)
                cdb_t = p.dma("scalar", cdb[:],
                              cddram.ap().unsqueeze(0).broadcast_to([128, 64]))
                prev_cdb = cdb_t
                p.wait("tensor", sc_tok)
                last_ev = None
                for st in range(2):
                    if last_ev is not None:
                        p.wait("tensor", last_ev)
                    tt = p.op("tensor", lambda e, st=st: e.transpose(
                        ptf[:, 0:64], acs[:, st * 128:(st + 1) * 128],
                        eyef[0:64, 0:64]), "pe")
                    p.wait("vector", tt)
                    p.op("vector", lambda e, st=st: e.tensor_copy(acst[:, st, :],
                                                                  ptf[:, 0:64]))
                    last_ev = p.op("vector", lambda e, st=st: e.tensor_scalar_mul(
                        nacst[:, st, :], acst[:, st, :], -1.0), "dve")
                    p.wait("tensor", last_ev)
                    tt = p.op("tensor", lambda e, st=st: e.transpose(
                        ptf[:, 0:64], dts[:, st * 128:(st + 1) * 128],
                        eyef[0:64, 0:64]), "pe")
                    p.wait("vector", tt)
                    last_ev = p.op("vector", lambda e, st=st: e.tensor_copy(
                        dtt[:, st, :], ptf[:, 0:64]), "dve")
                    p.wait("tensor", last_ev)
                    tt = p.op("tensor", lambda e, st=st: e.transpose(
                        ptf[:, 0:64], decf[:, st * 128:(st + 1) * 128],
                        eyef[0:64, 0:64]), "pe")
                    p.wait("vector", tt)
                    last_ev = p.op("vector", lambda e, st=st: e.tensor_copy(
                        dect[:, st, :], ptf[:, 0:64]), "dve")
                    p.wait("tensor", last_ev)
                    p.wait("tensor", conv_done)
                    tt = p.op("tensor", lambda e, cs0=cs0, st=st: e.transpose(
                        pt16[:], xpost[:, 32, st * 128:(st + 1) * 128],
                        eye16[:]), "pe")
                    p.wait("vector", tt)
                    last_ev = p.op("vector", lambda e, st=st: e.tensor_copy(
                        btb[:, st, :], pt16[:]), "dve")
                    p.wait("tensor", last_ev)
                if trunc <= 3:
                    continue
                Bc = xpost[:, 32, :]
                Cc = xpost[:, 33, :]
                for st in range(2):
                    pb = pp[2 + st]
                    bank_wait(2 + st)
                    mt = p.op("tensor", lambda e, Bc=Bc, Cc=Cc, st=st, pb=pb: e.matmul(
                        pb[:, 0:256], Bc[:, st * 128:(st + 1) * 128], Cc,
                        start=True, stop=True), "pe")
                    p.wait("vector", mt)
                    cb_t = p.op("vector", lambda e, st=st, pb=pb: e.tensor_tensor(
                        cbtm[:, st, :], pb[:, 0:256], tri[:, st, :], ALU.mult), "dve")
                    bk[2 + st] = cb_t
                if debug and ti == 0 and c == 0:
                    p.wait("gpsimd", cb_t)
                    p.dma("gpsimd", d_cbtm[:, :, :], cbtm[:])
                if trunc <= 4:
                    continue
                for g in range(16):
                    if band_free is not None:
                        p.wait("scalar", band_free)
                    if g == 0:
                        p.wait("scalar", rd0_t)
                        p.wait("scalar", rd1_t)
                    bb_t = p.dma("scalar", bcband[:],
                                 rdram.ap()[0, g * 4:(g + 1) * 4, :]
                                 .unsqueeze(0).broadcast_to([128, 4, 256]))
                    sd_t = p.dma("scalar", sdband[:],
                                 rdram.ap()[1, g * 4:(g + 1) * 4, :]
                                 .unsqueeze(0).broadcast_to([128, 4, 256]))
                    prev_bb, prev_sd = bb_t, sd_t
                    if xt8_free is not None:
                        p.wait("tensor", xt8_free)
                    p.wait("tensor", conv_done)
                    xt_t = None
                    for cg in range(2):
                        ct = 2 * g + cg
                        for st in range(2):
                            tt = p.op("tensor", lambda e, ct=ct, cs0=cs0, st=st:
                                      e.transpose(
                                          pt16[:],
                                          xpost[:, ct, st * 128:(st + 1) * 128],
                                          eye16[:]), "pe")
                            p.wait("vector", tt)
                            xt_t = p.op("vector", lambda e, cg=cg, st=st: e.tensor_copy(
                                xt8[:, cg, st, :], pt16[:]), "dve")
                            p.wait("tensor", xt_t)
                    for h in range(4 * g, 4 * g + 4):
                        par = h % 2
                        hg = h - 4 * g
                        cg = h // 2 - 2 * g
                        col = (h % 2) * 64
                        hc = slice(h * 64, (h + 1) * 64)
                        V = lambda fn, sem=None: p.op("vector", fn, sem)
                        gprev = ydx.get(par)
                        if gprev is not None:
                            p.wait("vector", gprev)
                        p.wait("vector", xt_t)
                        p.wait("vector", bb_t)
                        p.wait("vector", sd_t)
                        p.wait("vector", cdb_t)
                        if h == 0 and ti == 0 and c == 0:
                            p.wait("vector", ginit)
                        for st in range(2):
                            V(lambda e, par=par, st=st, cg=cg, col=col, h=h:
                              e.tensor_scalar_mul(
                                  xdt[par][:, st, :], xt8[:, cg, st, col:col + 64],
                                  dtt[:, st, h:h + 1]))
                            V(lambda e, par=par, st=st, h=h: e.tensor_scalar_mul(
                                xdd[par][:, st, :], xdt[par][:, st, :],
                                dect[:, st, h:h + 1]))
                            et = p.op("vector", lambda e, par=par, st=st, hg=hg,
                                      h=h: e.tensor_scalar(
                                          ecl[par][:], bcband[:, hg, :],
                                          nacst[:, st, h:h + 1], 0.0,
                                          ALU.add, ALU.min), "dve")
                            p.wait("scalar", et)
                            ex_t = p.op("scalar", lambda e, par=par: e.activation(
                                exb[par][:], ecl[par][:], AF.Exp), "act")
                            p.wait("vector", ex_t)
                            V(lambda e, par=par, st=st: e.tensor_tensor(
                                Mb[par][:, st, :], exb[par][:], cbtm[:, st, :],
                                ALU.mult))
                        V(lambda e, par=par, hg=hg, Cc=Cc: e.tensor_tensor(
                            csd[par][:], Cc, sdband[:, hg, :], ALU.mult))
                        dv_t = p.op("vector", lambda e, par=par, hc=hc: e.tensor_copy(
                            S16[par][:], Sbuf[:, hc]), "dve")
                        if debug and ti == 0 and c == 0 and h == 0:
                            p.wait("gpsimd", dv_t)
                            p.dma("gpsimd", d_Mb0[:, :, :], Mb[par][:])
                            p.dma("gpsimd", d_xdt0[:, :, :], xdt[par][:])
                            p.dma("gpsimd", d_dtt0[:, :, :], dtt[:])
                            dmp = p.dma("gpsimd", d_xt80[:, :, :, :], xt8[:])
                            p.wait("vector", dmp)
                        ydb = pp[4 + par]
                        stb = pp[0 + par]
                        bank_wait(4 + par)
                        bank_wait(0 + par)
                        p.wait("tensor", dv_t)
                        first = True
                        for lt in range(2):
                            for st in range(2):
                                p.op("tensor", lambda e, par=par, lt=lt, st=st,
                                     ydb=ydb, first=first: e.matmul(
                                         ydb[:, lt * 64:(lt + 1) * 64],
                                         Mb[par][:, st, lt * 128:(lt + 1) * 128],
                                         xdt[par][:, st, :],
                                         start=first, stop=False))
                                first = False
                        yd_t = None
                        for lt in range(2):
                            yd_t = p.op("tensor", lambda e, par=par, lt=lt, ydb=ydb:
                                        e.matmul(
                                            ydb[:, lt * 64:(lt + 1) * 64],
                                            csd[par][:, lt * 128:(lt + 1) * 128],
                                            S16[par][:],
                                            start=False, stop=(lt == 1)),
                                        "pe" if lt == 1 else None)
                        st_t = None
                        for st in range(2):
                            st_t = p.op("tensor", lambda e, par=par, st=st, stb=stb:
                                        e.matmul(stb[:, 0:64], btb[:, st, :],
                                                 xdd[par][:, st, :],
                                                 start=(st == 0), stop=(st == 1)),
                                        "pe" if st == 1 else None)
                        ydx[par] = st_t
                        p.wait("vector", st_t)
                        V(lambda e, hc=hc, h=h: e.tensor_scalar_mul(
                            Sbuf[:, hc], Sbuf[:, hc], cdb[:, h:h + 1]))
                        su_t = V(lambda e, hc=hc, stb=stb: e.tensor_tensor(
                            Sbuf[:, hc], Sbuf[:, hc], stb[:, 0:64], ALU.add), "dve")
                        bk[0 + par] = su_t
                        ye = None
                        for lt in range(2):
                            ye = V(lambda e, cg=cg, lt=lt, col=col, hc=hc, ydb=ydb,
                                   h=h: e.scalar_tensor_tensor(
                                       ybuf[:, lt, hc], xt8[:, cg, lt, col:col + 64],
                                       D_b[:, h:h + 1], ydb[:, lt * 64:(lt + 1) * 64],
                                       ALU.mult, ALU.add), "dve")
                        bk[4 + par] = ye
                    xt8_free = st_t
                    band_free = ye
                if debug and ti == 0 and c == 0:
                    p.wait("gpsimd", p.tok("dve"))
                    p.dma("gpsimd", d_ybuf[:, :, :], ybuf[:])
                if trunc <= 5:
                    continue
                for zg in range(8):
                    zfirst = [True, True]
                    for q in range(2):
                        bank_wait(2 + q)
                    for cq in range(4):
                        zct = 4 * zg + cq
                        bwi = cq % 2
                        if wib_free[bwi] is not None:
                            p.wait("sync", wib_free[bwi])
                        ldz = p.dma("sync", wibuf[bwi][:], wi_blk.ap()[zct, :, :, :])
                        p.wait("tensor", ldz)
                        mt = None
                        for lt in range(2):
                            pb = pp[2 + lt]
                            for kc in range(KC):
                                mt = p.op("tensor", lambda e, bwi=bwi, kc=kc, lt=lt,
                                          cq=cq, pb=pb, cs0=cs0, zf=zfirst[lt]:
                                          e.matmul(
                                              pb[:, cq * 128:(cq + 1) * 128],
                                              s["x16f"][:, kc,
                                                        cs0 + lt * 128:cs0 + (lt + 1) * 128],
                                              wibuf[bwi][:, kc, :],
                                              start=(zf and kc == 0),
                                              stop=(cq == 3 and kc == KC - 1)),
                                          "pe" if (lt == 1 and kc == KC - 1) else None)
                            zfirst[lt] = False
                        wib_free[bwi] = mt
                        last_z_mt = mt
                    zmm = p.tok("pe")
                    for lt in range(2):
                        p.wait("scalar", zmm)
                        if ze_t is not None:
                            p.wait("scalar", ze_t)
                        sl_t = p.op("scalar", lambda e, lt=lt: e.activation(
                            zsil[:], pp[2 + lt][:], AF.Silu), "act")
                        bk[2 + lt] = sl_t
                        if debug and ti == 0 and c == 0 and zg == 0:
                            p.wait("gpsimd", sl_t)
                            dmpz = p.dma("gpsimd", d_zsil[:, lt, :], zsil[:])
                            p.wait("scalar", dmpz)
                        p.wait("vector", sl_t)
                        ze_t = p.op("vector", lambda e, lt=lt, zg=zg: e.tensor_tensor(
                            ybuf[:, lt, zg * 512:(zg + 1) * 512],
                            ybuf[:, lt, zg * 512:(zg + 1) * 512], zsil[:], ALU.mult),
                            "dve")
                if debug and ti == 0 and c == 0:
                    p.wait("gpsimd", ze_t)
                    p.dma("gpsimd", d_ygate[:, :, :], ybuf[:])
                if trunc <= 6:
                    continue
                yt_t = None
                for lt in range(2):
                    p.wait("scalar", ze_t)
                    acc_tok = None
                    for q8 in range(8):
                        tq = p.op("scalar", lambda e, lt=lt, q8=q8: e.activation(
                            s["t_ln"][:, :], ybuf[:, lt, q8 * 512:(q8 + 1) * 512],
                            AF.Square, accum_out=rsum[q8 % 2][:, :]), "act")
                        p.wait("vector", tq)
                        if q8 == 0:
                            acc_tok = p.op("vector", lambda e: e.tensor_copy(
                                rstd[:], rsum[0][:, :]), "dve")
                        else:
                            acc_tok = p.op("vector", lambda e, q8=q8: e.tensor_tensor(
                                rstd[:], rstd[:], rsum[q8 % 2][:, :], ALU.add), "dve")
                        if q8 < 7:
                            p.wait("scalar", acc_tok)
                    p.wait("scalar", acc_tok)
                    p.op("scalar", lambda e: e.activation(
                        rstd[:], rstd[:], AF.Ln, bias=c_eps128[:, :],
                        scale=1.0 / 4096.0))
                    rst_tok = p.op("scalar", lambda e: e.activation(
                        rstd[:], rstd[:], AF.Exp, scale=-0.5), "act")
                    p.wait("vector", rst_tok)
                    yn_tok = p.op("vector", lambda e, lt=lt: e.tensor_scalar_mul(
                        ybuf[:, lt, :], ybuf[:, lt, :], rstd[:, :]), "dve")
                    p.wait("tensor", yn_tok)
                    for kc2 in range(32):
                        if yt_t is not None:
                            p.wait("tensor", yt_t)
                        tt = p.op("tensor", lambda e, lt=lt, kc2=kc2: e.transpose(
                            pt16[:], ybuf[:, lt, kc2 * 128:(kc2 + 1) * 128], eye16[:]),
                            "pe")
                        p.wait("vector", tt)
                        yt_t = p.op("vector", lambda e, lt=lt, kc2=kc2:
                                    e.tensor_scalar_mul(
                                        ynt[:, kc2, lt * 128:(lt + 1) * 128], pt16[:],
                                        nwt[:, kc2:kc2 + 1]), "dve")
                if debug and ti == 0 and c == 0:
                    p.wait("gpsimd", yt_t)
                    p.dma("gpsimd", d_ynt[:, :, :], ynt[:])
                if trunc <= 7:
                    continue
                for dmt in range(16):
                    if wob_free is not None:
                        p.wait("sync", wob_free)
                    ldo = p.dma("sync", wobuf[:], wo_blk.ap()[dmt, :, :, :])
                    pb = pp[dmt % 2]
                    bank_wait(dmt % 2)
                    p.wait("tensor", ldo)
                    p.wait("tensor", yt_t)
                    mt = None
                    for kc2 in range(32):
                        mt = p.op("tensor", lambda e, kc2=kc2, pb=pb: e.matmul(
                            pb[:, 0:256], wobuf[:, kc2, :], ynt[:, kc2, :],
                            start=(kc2 == 0), stop=(kc2 == 31)),
                            "pe" if kc2 == 31 else None)
                    wob_free = mt
                    p.wait("scalar", mt)
                    oe = p.op("scalar", lambda e, dmt=dmt, csl=csl, pb=pb: e.copy(
                        obuf[:, dmt, csl], pb[:, 0:256]), "act")
                    bk[dmt % 2] = oe
            o_done = p.tok("act")
            if debug and ti == 0:
                p.wait("gpsimd", o_done)
                p.dma("gpsimd", d_obuf[:, :, :], obuf[:])

            s["stats_free"] = last_z_mt
            for bi in range(4):
                bank_wait(bi)
            st2 = emit_ln_stats(p, s, obuf, KC, pp[0][0:1, :], pp[1][0:1, :], o_done)
            r2 = emit_ln_vec(p, s, pp[0][0:1, :], pp[1][0:1, :], st2, D,
                             s["v_m"], s["v_r"])
            mb2 = emit_bcast2(p, s, s["v_m"], s["v_r"], r2, pp[2], pp[3])
            bk[0] = bk[1] = bk[2] = bk[3] = mb2
            p.wait("vector", mb2)
            ot_free = [None, None]
            for kc in range(KC):
                ob = kc % 2
                if ot_free[ob] is not None:
                    p.wait("vector", ot_free[ob])
                p.op("vector", lambda e, kc=kc: e.tensor_tensor(
                    s["t_ln"][:], obuf[:, kc, :], s["m_b"][:], ALU.subtract))
                p.op("vector", lambda e: e.tensor_tensor(
                    s["t_ln"][:], s["t_ln"][:], s["r_b"][:], ALU.mult))
                ot = p.op("vector", lambda e, kc=kc, ob=ob: e.tensor_tensor(
                    s["out2"][ob][:], s["t_ln"][:], x[:, kc, :], ALU.add), "dve")
                p.wait("scalar", ot)
                ot_free[ob] = p.dma(
                    "scalar", x1T.ap().rearrange("(c p) t -> p c t", p=128)
                    [:, kc, t0:t0 + NT], s["out2"][ob][:])
            x_free = ot_free[(KC - 1) % 2]

        p.qwait_all("gpsimd", "scalar")
        p.qwait_all("gpsimd", "sync")
        p.qwait_all("gpsimd", "gpsimd")
        if fused is None:
            p.build()
    return nc if fused is None else x1T


def mamba_in_maps(ins, ntiles=16):
    u = np.asarray(ins["L0_attn_u"])
    maps = []
    for c in range(NCORES):
        seq = c % 4
        maps.append(dict(
            uT=np.ascontiguousarray(u[seq].T),
            wi=np.asarray(ins["L0_attn_in_proj_kernel"]),
            wo=np.asarray(ins["L0_attn_out_proj_kernel"]),
            cw=np.asarray(ins["L0_attn_conv1d_weight"]),
            cbv=np.ascontiguousarray(
                np.asarray(ins["L0_attn_conv1d_bias"]).reshape(34, 128).T),
            dtb=np.asarray(ins["L0_attn_dt_bias"]),
            alog=np.asarray(ins["L0_attn_A_log"]),
            dvec=np.asarray(ins["L0_attn_D"]),
            nwv=np.ascontiguousarray(
                np.asarray(ins["L0_attn_norm_weight"]).reshape(32, 128).T),
        ))
    return maps


def mamba_gather(results):
    return np.stack([results[c]["x1T"].T for c in range(4)], axis=0)
